# version 21
# baseline (speedup 1.0000x reference)
import numpy as np
import concourse.bass as bass
import concourse.mybir as mybir
from concourse.bass_utils import run_bass_kernel_spmd
from contextlib import ExitStack

F32 = mybir.dt.float32
BF16 = mybir.dt.bfloat16
I32 = mybir.dt.int32
AF = mybir.ActivationFunctionType
ALU = mybir.AluOpType
AX = mybir.AxisListType

P = 128
H = 1024
HC = 8
S = 4096
T = 256
NT = S // T
FF = 4096
FC = 32
NSLOT = 2048
EPS = 1e-5

_NC_CACHE = [None]


def build_nc(debug=False):
    nc = bass.Bass("TRN2", target_bir_lowering=False, detect_race_conditions=False)

    x_d = nc.dram_tensor("x", [S, H], F32, kind="ExternalInput")
    wq_d = nc.dram_tensor("wq", [H, H], F32, kind="ExternalInput")
    wk_d = nc.dram_tensor("wk", [H, H], F32, kind="ExternalInput")
    wv_d = nc.dram_tensor("wv", [H, H], F32, kind="ExternalInput")
    wo_d = nc.dram_tensor("wo", [H, H], F32, kind="ExternalInput")
    bq_d = nc.dram_tensor("bq", [H], F32, kind="ExternalInput")
    bk_d = nc.dram_tensor("bk", [H], F32, kind="ExternalInput")
    bv_d = nc.dram_tensor("bv", [H], F32, kind="ExternalInput")
    bo_d = nc.dram_tensor("bo", [H], F32, kind="ExternalInput")
    gm_d = nc.dram_tensor("gm", [H], F32, kind="ExternalInput")
    bt_d = nc.dram_tensor("bt", [H], F32, kind="ExternalInput")
    f1q_d = nc.dram_tensor("f1q", [H, FF], I32, kind="ExternalInput")
    f1s_d = nc.dram_tensor("f1s", [HC, FF], F32, kind="ExternalInput")
    f1b_d = nc.dram_tensor("f1b", [FF], F32, kind="ExternalInput")
    f2q_d = nc.dram_tensor("f2q", [FF, H], I32, kind="ExternalInput")
    f2s_d = nc.dram_tensor("f2s", [FC, H], F32, kind="ExternalInput")
    f2b_d = nc.dram_tensor("f2b", [H], F32, kind="ExternalInput")
    out_d = nc.dram_tensor("out", [S, H], F32, kind="ExternalOutput")
    sck = "ExternalOutput" if debug else "Internal"
    xT_d = nc.dram_tensor("xT", [NT * P, 2048], F32, kind=sck)
    qT_d = nc.dram_tensor("qT", [NT * P, 2048], F32, kind=sck)
    taps = {}
    if debug:
        for nm, shape in (("dkv", [P, HC]), ("dks", [P, HC]), ("dvv", [P, 2048]),
                          ("dr1", [P, 2048]), ("dx1", [P, 2048]), ("dh", [P, 2048]),
                          ("dy", [P, 2048]), ("dtp", [P, 2048]),
                          ("dxm", [P, 2048]), ("dxf", [P, 2048]),
                          ("dkt", [P, 2048]), ("dvt", [P, 2048]),
                          ("dk15", [P, 2048]), ("dkp15", [P, HC]), ("dks15", [P, HC])):
            taps[nm] = nc.dram_tensor(nm, shape, F32, kind="ExternalOutput")
    sw_d = {}
    for nm in ("wq", "wk", "wv", "wo"):
        for hl in ("h", "l"):
            sw_d[nm + hl] = nc.dram_tensor("s" + nm + hl, [P, 8192], BF16, kind=sck)
    w1h_d = nc.dram_tensor("w1h", [HC * P, 8192], BF16, kind=sck)
    w1l_d = nc.dram_tensor("w1l", [HC * P, 8192], BF16, kind=sck)
    w2h_d = nc.dram_tensor("w2h", [4 * P, 8192], BF16, kind=sck)
    w2l_d = nc.dram_tensor("w2l", [4 * P, 8192], BF16, kind=sck)

    with ExitStack() as stack:
        sb = lambda name, shape, dt: stack.enter_context(nc.sbuf_tensor(name, shape, dt))
        ident = sb("ident", [P, P], F32)
        ones = sb("ones", [P, P], F32)
        cn = sb("cn", [P, 2], F32)
        bqs = sb("bqs", [P, HC], F32)
        bks = sb("bks", [P, HC], F32)
        bvs = sb("bvs", [P, HC], F32)
        bos = sb("bos", [P, HC], F32)
        gms = sb("gms", [P, HC], F32)
        bts = sb("bts", [P, HC], F32)
        b1s = sb("b1s", [P, FC], F32)
        b2s = sb("b2s", [P, HC], F32)
        b2q = sb("b2q", [P, HC], F32)
        kva = sb("kva", [P, HC], F32)
        ksa = sb("ksa", [P, HC], F32)
        mus = sb("mus", [P, T], F32)
        sds = sb("sds", [P, T], F32)
        msq = sb("msq", [P, T], F32)
        rss = sb("rss", [P, T], F32)
        tm2 = sb("tm2", [P, T], F32)
        tm3 = sb("tm3", [P, T], F32)
        kbuf = sb("kbuf", [P, HC * NT], F32)
        vbuf = sb("vbuf", [P, HC * NT], F32)
        dqb = sb("dqb", [P, 1536], F32)
        dsr = sb("dsr", [1, 1536], F32)
        dsc = sb("dsc", [P, 512], F32)
        sq2 = dsc
        apool = sb("apool", [P, 9 * NSLOT], F32)
        bh = [sb("bh0", [P, NSLOT], BF16), sb("bh1", [P, NSLOT], BF16)]
        bl = [sb("bl0", [P, NSLOT], BF16), sb("bl1", [P, NSLOT], BF16)]
        b0 = bh[0]
        b1t = bl[0]
        y0 = sb("y0", [P, FC * T], BF16)
        y1 = sb("y1", [P, FC * T], BF16)
        ws = [sb("ws0", [P, 16384], BF16), sb("ws1", [P, 16384], BF16)]
        pb = [stack.enter_context(nc.psum_tensor(f"pb{i}", [P, 512], F32)) for i in range(8)]
        semnames = (["T", "V", "S", "G"] + [f"SQ{i}" for i in range(8)]
                    + [f"GQ{i}" for i in range(4)])
        sems = {k: stack.enter_context(nc.semaphore("sem" + k)) for k in semnames}
        qpool = {"SQ": [f"SQ{i}" for i in range(8)], "GQ": [f"GQ{i}" for i in range(4)]}
        qeng = {"SQ": "sync", "GQ": "gpsimd"}
        qidx = {"SQ": 0, "GQ": 0}
        block = stack.enter_context(nc.Block())

        def slot(i, w=NSLOT):
            return apool[:, i * NSLOT:i * NSLOT + w]

        ops = {e: [] for e in ("tensor", "vector", "scalar", "sync", "gpsimd")}
        cnt = dict.fromkeys(sems, 0)
        last_wait = {}

        def emit(eng, fn, inc=None, n=1):
            ops[eng].append(("op", fn, inc, n))
            if inc:
                cnt[inc] += n
                return cnt[inc]
            return None

        def wait(eng, sem, val):
            if val is None or val <= 0:
                return
            key = (eng, sem)
            if last_wait.get(key, 0) >= val:
                return
            last_wait[key] = val
            ops[eng].append(("wait", sem, val))

        bank_free = [(None, 0)] * 8

        def use_bank(b):
            s, c = bank_free[b]
            if s:
                wait("tensor", s, c)

        ws_last_mm = [0, 0]
        load_idx = [0]

        def dma(eng, queue, out, in_, slow=False):
            sm = qpool[queue][qidx[queue] % len(qpool[queue])]
            qidx[queue] += 1
            wait(eng, sm, cnt[sm])
            val = emit(eng, (lambda o=out, i=in_, s=slow: (
                lambda e: e.dma_start(out=o, in_=i, allow_slow_non_contiguous=s)))(), sm, 16)
            return (sm, val)

        def wait_tok(eng, tok):
            if tok:
                if isinstance(tok, list):
                    for t_ in tok:
                        wait_tok(eng, t_)
                else:
                    wait(eng, tok[0], tok[1])

        def qsnap(queue):
            return [(sm, cnt[sm]) for sm in qpool[queue] if cnt[sm] > 0]

        def tap(nm, src_ap, sem, val):
            if debug and nm in taps:
                if isinstance(sem, tuple):
                    wait_tok("gpsimd", sem)
                else:
                    wait("gpsimd", sem, val)
                cols = taps[nm].shape[1]
                dma("gpsimd", "GQ", bass.AP(taps[nm], 0, [[cols, P], [1, cols]]), src_ap)

        def dr_tiles(handle, C, r0, nchunk, c0, width):
            return bass.AP(handle, r0 * C + c0, [[C, P], [P * C, nchunk], [1, width]])

        def bc_tok(handle, free):
            return bass.AP(handle, 0, [[free, P], [1, free], [0, T]])

        def bc_chunk(handle):
            return bass.AP(handle, 0, [[T, P], [0, HC], [1, T]])

        def r3(ap_, c):
            return ap_.rearrange("p (c t) -> p c t", c=c)

        g0 = emit("gpsimd", lambda e: e.memset(ident[:, :], 0.0), "G")
        wait("gpsimd", "G", g0)
        emit("gpsimd", lambda e: e.affine_select(
            out=ident[:, :], in_=ident[:, :], compare_op=ALU.not_equal,
            fill=1.0, base=0, pattern=[[-1, P]], channel_multiplier=1))
        emit("gpsimd", lambda e: e.memset(ones[:, :], 1.0))
        emit("gpsimd", lambda e: e.memset(cn[:, 0:1], EPS))
        g_consts = emit("gpsimd", lambda e: e.memset(cn[:, 1:2], EPS / 4.0), "G")

        for dst, src in ((bqs, bq_d), (bks, bk_d), (bvs, bv_d), (bos, bo_d),
                         (gms, gm_d), (bts, bt_d), (b2s, f2b_d)):
            dma("sync", "SQ", dst[:, :], bass.AP(src, 0, [[1, P], [P, HC]]), slow=True)
        dma("sync", "SQ", b1s[:, :], bass.AP(f1b_d, 0, [[1, P], [P, FC]]), slow=True)
        sq_params = qsnap("SQ")
        wait_tok("vector", sq_params)
        emit("vector", lambda e: e.tensor_scalar(out=b2q[:, :], in0=b2s[:, :],
                                                 scalar1=0.25, scalar2=None, op0=ALU.mult))
        v_init = cnt["V"]

        v_prev = {}
        gq_half_store = {}
        hidx = [0]

        def attn_split(wsrc, nm, half):
            i = hidx[0]
            hidx[0] += 1
            par = i % 2
            ybuf = y0 if par == 0 else y1
            yh = ybuf[:, 0:4096]
            yl = ybuf[:, 4096:8192]
            wait("sync", "V", v_prev.get(i - 2, 0))
            ld1 = dma("sync", "SQ", r3(slot(5 + par, 2048), 2),
                      dr_tiles(wsrc, H, half * 512, 2, 0, H))
            ld2 = dma("sync", "SQ", r3(slot(7 + par, 2048), 2),
                      dr_tiles(wsrc, H, half * 512 + 256, 2, 0, H))
            wait_tok("vector", ld1)
            wait_tok("vector", ld2)
            wait_tok("vector", gq_half_store.get(i - 2))
            emit("vector", lambda e, par=par, yh=yh: e.tensor_copy(
                out=yh[:, 0:2048], in_=slot(5 + par, 2048)))
            emit("vector", lambda e, par=par, yh=yh: e.tensor_copy(
                out=yh[:, 2048:4096], in_=slot(7 + par, 2048)))
            emit("vector", lambda e, par=par, yh=yh, yl=yl: e.tensor_tensor(
                out=yl[:, 0:2048], in0=slot(5 + par, 2048), in1=yh[:, 0:2048],
                op=ALU.subtract))
            vs = emit("vector", lambda e, par=par, yh=yh, yl=yl: e.tensor_tensor(
                out=yl[:, 2048:4096], in0=slot(7 + par, 2048), in1=yh[:, 2048:4096],
                op=ALU.subtract), "V")
            v_prev[i] = vs
            wait("gpsimd", "V", vs)
            t1 = dma("gpsimd", "GQ",
                     bass.AP(sw_d[nm + "h"], half * 4096, [[8192, P], [1, 4096]]), yh)
            t2 = dma("gpsimd", "GQ",
                     bass.AP(sw_d[nm + "l"], half * 4096, [[8192, P], [1, 4096]]), yl)
            gq_half_store[i] = t2
            sw_toks.setdefault(nm, []).extend([t1, t2])

        sw_toks = {}
        for wsrc, nm in ((wv_d, "wv"), (wk_d, "wk"), (wq_d, "wq"), (wo_d, "wo")):
            for half in range(2):
                attn_split(wsrc, nm, half)
        gq_prev_store = [gq_half_store[hidx[0] - 2], gq_half_store[hidx[0] - 1]]

        dq_slices = []
        for g in range(HC):
            for j in range(FF // 512):
                dq_slices.append((f1q_d, f1s_d, w1h_d, w1l_d, g, j, FF))
        for g in range(FC):
            for j in range(H // 512):
                dq_slices.append((f2q_d, f2s_d, w2h_d, w2l_d, g, j, H))
        dq_state = {"load": 0, "comp": 0, "v_par": {}, "t_par": {},
                    "g_store": gq_prev_store,
                    "lq": {}, "ls": {}, "done": False, "laststore": None}
        DQDEPTH = 3

        def dq_load(i):
            qd, sd, wh_d_, wl_d_, g, j, ncols = dq_slices[i]
            par = i % DQDEPTH
            wait("gpsimd", "T", dq_state["t_par"].get(i - DQDEPTH, 0))
            dq_state["ls"][i] = dma(
                "gpsimd", "GQ", dsr[0:1, par * 512:(par + 1) * 512],
                bass.AP(sd, g * ncols + j * 512, [[ncols, 1], [1, 512]]))
            wait("gpsimd", "V", dq_state["v_par"].get(i - DQDEPTH, 0))
            dq_state["lq"][i] = dma(
                "gpsimd", "GQ", dqb[:, par * 512:(par + 1) * 512],
                bass.AP(qd, g * P * ncols + j * 512, [[ncols, P], [1, 512]]))

        def dq_pump(n):
            while dq_state["load"] < min(dq_state["comp"] + DQDEPTH, len(dq_slices)):
                dq_load(dq_state["load"])
                dq_state["load"] += 1
            for _ in range(n):
                i = dq_state["comp"]
                if i >= len(dq_slices):
                    dq_state["done"] = True
                    return
                dq_state["comp"] = i + 1
                qd, sd, wh_d_, wl_d_, g, j, ncols = dq_slices[i]
                par = i % DQDEPTH
                nslc = ncols // 512
                bkn = 4 + i % 2
                use_bank(bkn)
                wait_tok("tensor", dq_state["ls"][i])
                wait("tensor", "G", g_consts)
                tcnt = emit("tensor", lambda e, par=par, bkn=bkn: e.matmul(
                    pb[bkn][:, 0:512], ones[0:1, :],
                    dsr[0:1, par * 512:(par + 1) * 512],
                    start=True, stop=True), "T")
                dq_state["t_par"][i] = tcnt
                wait("vector", "T", tcnt)
                wait_tok("vector", dq_state["lq"][i])
                if j == 0:
                    wait_tok("vector", dq_state["g_store"])
                emit("vector", lambda e, par=par, bkn=bkn: e.scalar_tensor_tensor(
                    out=dsc[:, :], in0=dqb[:, par * 512:(par + 1) * 512],
                    scalar=-128.0, in1=pb[bkn][:, 0:512],
                    op0=ALU.add, op1=ALU.mult))
                emit("vector", lambda e, j=j: e.tensor_copy(
                    out=y0[:, j * 512:(j + 1) * 512], in_=dsc[:, :]))
                vcnt = emit("vector", lambda e, j=j: e.tensor_tensor(
                    out=y0[:, 4096 + j * 512: 4096 + (j + 1) * 512], in0=dsc[:, :],
                    in1=y0[:, j * 512:(j + 1) * 512], op=ALU.subtract), "V")
                dq_state["v_par"][i] = vcnt
                bank_free[bkn] = ("V", vcnt)
                if j == nslc - 1:
                    wait("gpsimd", "V", vcnt)
                    if ncols == FF:
                        for mg in range(4):
                            dma("gpsimd", "GQ",
                                bass.AP(wh_d_, mg * P * 8192 + g * 1024,
                                        [[8192, P], [1, 1024]]),
                                y0[:, mg * 1024:(mg + 1) * 1024])
                            dq_state["g_store"] = dma(
                                "gpsimd", "GQ",
                                bass.AP(wl_d_, mg * P * 8192 + g * 1024,
                                        [[8192, P], [1, 1024]]),
                                y0[:, 4096 + mg * 1024:4096 + (mg + 1) * 1024])
                    else:
                        kg2, kc2 = g // 8, g % 8
                        dma("gpsimd", "GQ",
                            bass.AP(wh_d_, kg2 * P * 8192 + kc2 * 1024,
                                    [[8192, P], [1, 1024]]),
                            y0[:, 0:1024])
                        dq_state["g_store"] = dma(
                            "gpsimd", "GQ",
                            bass.AP(wl_d_, kg2 * P * 8192 + kc2 * 1024,
                                    [[8192, P], [1, 1024]]),
                            y0[:, 4096:4096 + 1024])
                    dq_state["laststore"] = dq_state["g_store"]

        gq_phase0 = qsnap("GQ")
        v_phase0 = cnt["V"]

        p1_loads = {}

        def p1_x_load(t):
            xs = slot(t % 2)
            wait("sync", "T", p1_loads.get(t - 2, {}).get("x_mm_cnt", 0))
            p1_loads.setdefault(t, {})["x"] = dma(
                "sync", "SQ", r3(xs, 2),
                bass.AP(x_d, t * T * H, [[H, P], [P * H, 2], [1, H]]))

        p1_wseq = [(pr, nm) for pr in range(NT // 2) for nm in ("wv", "wk", "wq")]
        p1_wcur = [0]

        def p1_pump():
            if p1_wcur[0] >= len(p1_wseq):
                return
            pr, nm = p1_wseq[p1_wcur[0]]
            p1_wcur[0] += 1
            sl = load_idx[0] % 2
            load_idx[0] += 1
            wait_tok("sync", sw_toks[nm])
            wait("sync", "T", ws_last_mm[sl])
            dma("sync", "SQ", ws[sl][:, 0:8192],
                bass.AP(sw_d[nm + "h"], 0, [[8192, P], [1, 8192]]))
            d = p1_loads.setdefault(2 * pr, {})
            d[nm] = dma("sync", "SQ", ws[sl][:, 8192:16384],
                        bass.AP(sw_d[nm + "l"], 0, [[8192, P], [1, 8192]]))
            d[nm + "_slot"] = sl

        p1_x_load(0)
        p1_x_load(1)
        p1_pump()
        p1_pump()

        def p1_head(t):
            L = p1_loads[t]
            xs = slot(t % 2)
            xf = slot(2)
            if t == 0 and debug:
                tap("dxm", xs, L["x"], None)
            wait_tok("tensor", L["x"])
            wait("tensor", "G", g_consts)
            wait_tok("vector", p1_loads.get(t - 1, {}).get("xT"))
            for fc in range(HC):
                for s_ in range(2):
                    bkn = 6 + (2 * fc + s_) % 2
                    use_bank(bkn)
                    c = emit("tensor", lambda e, fc=fc, s_=s_, bkn=bkn, xs=xs: e.transpose(
                        pb[bkn][:, 0:P], xs[:, s_ * H + fc * P: s_ * H + (fc + 1) * P],
                        ident[:, :]), "T")
                    wait("vector", "T", c)
                    vc = emit("vector", lambda e, fc=fc, s_=s_, bkn=bkn, xf=xf: e.tensor_copy(
                        out=xf[:, fc * T + s_ * P: fc * T + s_ * P + P],
                        in_=pb[bkn][:, 0:P]), "V")
                    bank_free[bkn] = ("V", vc)
            L["x_mm_cnt"] = cnt["T"]
            v_xf = cnt["V"]
            if t == 0 and debug:
                tap("dxf", xf, "V", v_xf)
            wait("gpsimd", "V", v_xf)
            L["xT"] = dma("gpsimd", "GQ",
                          bass.AP(xT_d, t * P * 2048, [[2048, P], [1, 2048]]), xf)
            wait("vector", "T", cnt["T"])
            emit("vector", lambda e, xf=xf, t=t: e.tensor_copy(out=bh[t % 2][:, :], in_=xf))
            L["split"] = emit("vector", lambda e, xf=xf, t=t: e.tensor_tensor(
                out=bl[t % 2][:, :], in0=xf, in1=bh[t % 2][:, :], op=ALU.subtract), "V")

        p1_head(0)
        p1_head(1)

        def p1_proj(t, proj, nm, bias):
            L0 = p1_loads[2 * (t // 2)]
            L = p1_loads[t]
            bhx = bh[t % 2]
            blx = bl[t % 2]
            sl = L0[nm + "_slot"]
            wait_tok("tensor", L0[nm])
            wait("tensor", "V", L["split"])
            wh = ws[sl][:, 0:8192]
            wl = ws[sl][:, 8192:16384]
            for mc in range(HC):
                bkn = mc % 4
                use_bank(bkn)
                last = None
                for kc in range(HC):
                    for pi, (wop, xop) in enumerate(((wh, bhx), (wh, blx), (wl, bhx))):
                        st = (kc == 0 and pi == 0)
                        sp = (kc == HC - 1 and pi == 2)
                        last = emit("tensor", lambda e, wop=wop, xop=xop, kc=kc, mc=mc, bkn=bkn, st=st, sp=sp: e.matmul(
                            pb[bkn][:, 0:T],
                            wop[:, kc * H + mc * P: kc * H + (mc + 1) * P],
                            xop[:, kc * T:(kc + 1) * T], start=st, stop=sp),
                            "T" if sp else None)
                ws_last_mm[sl] = cnt["T"]
                dstslot = {"v": slot(5), "k": slot(3), "q": slot(4)}[proj]
                wait("vector", "T", last)
                vc = emit("vector", lambda e, mc=mc, bkn=bkn, dstslot=dstslot, bias=bias: e.tensor_scalar(
                    out=dstslot[:, mc * T:(mc + 1) * T], in0=pb[bkn][:, 0:T],
                    scalar1=bias[:, mc:mc + 1], scalar2=None, op0=ALU.add), "V")
                bank_free[bkn] = ("V", vc)
            dq_pump(3)

        def p1_kbulk(t):
            L = p1_loads[t]
            kb = slot(3)
            wait_tok("vector", p1_loads.get(t - 1, {}).get("qT"))
            emit("vector", lambda e: e.tensor_scalar(
                out=slot(6), in0=kb, scalar1=0.0, scalar2=None, op0=ALU.min))
            vm0 = emit("vector", lambda e: e.tensor_scalar(
                out=slot(8), in0=kb, scalar1=0.0, scalar2=None, op0=ALU.max), "V")
            wait("scalar", "V", vm0)
            sE = emit("scalar", lambda e: e.activation(
                out=slot(7), in_=slot(6), func=AF.Exp), "S")
            wait("vector", "S", sE)
            emit("vector", lambda e: e.tensor_tensor(
                out=kb, in0=slot(7), in1=slot(8), op=ALU.add))
            emit("vector", lambda e, t=t: e.tensor_reduce(
                out=bass.AP(kbuf, t, [[HC * NT, P], [NT, HC]]),
                in_=r3(kb, HC), axis=AX.X, op=ALU.add))
            emit("vector", lambda e: e.tensor_tensor(
                out=slot(8), in0=kb, in1=slot(5), op=ALU.mult))
            vkv = emit("vector", lambda e, t=t: e.tensor_reduce(
                out=bass.AP(vbuf, t, [[HC * NT, P], [NT, HC]]),
                in_=r3(slot(8), HC), axis=AX.X, op=ALU.add), "V")
            if t == 0 and debug:
                tap("dkt", kb, "V", vkv)
                tap("dvt", slot(5), "V", vkv)

        def p1_qbulk(t):
            L = p1_loads[t]
            qb1 = slot(4)
            qdst = slot(6)
            wait_tok("vector", p1_loads.get(t - 1, {}).get("qT"))
            emit("vector", lambda e, qb1=qb1: e.tensor_scalar(
                out=slot(6), in0=qb1, scalar1=0.0, scalar2=None, op0=ALU.min))
            vm0 = emit("vector", lambda e, qb1=qb1: e.tensor_scalar(
                out=slot(8), in0=qb1, scalar1=0.0, scalar2=None, op0=ALU.max), "V")
            wait("scalar", "V", vm0)
            sE = emit("scalar", lambda e: e.activation(
                out=slot(7), in_=slot(6), func=AF.Exp), "S")
            wait("vector", "S", sE)
            vq = emit("vector", lambda e, qdst=qdst: e.tensor_tensor(
                out=qdst, in0=slot(7), in1=slot(8), op=ALU.add), "V")
            wait("gpsimd", "V", vq)
            L["qT"] = dma("gpsimd", "GQ",
                          bass.AP(qT_d, t * P * 2048, [[2048, P], [1, 2048]]),
                          qdst)

        for pr in range(NT // 2):
            t0_, t1_ = 2 * pr, 2 * pr + 1
            p1_proj(t0_, "v", "wv", bvs)
            p1_proj(t0_, "k", "wk", bks)
            p1_kbulk(t0_)
            if t0_ + 2 < NT:
                p1_x_load(t0_ + 2)
            p1_proj(t1_, "v", "wv", bvs)
            p1_pump()
            p1_proj(t1_, "k", "wk", bks)
            p1_pump()
            p1_kbulk(t1_)
            if t1_ + 2 < NT:
                p1_x_load(t1_ + 2)
            p1_proj(t0_, "q", "wq", bqs)
            p1_qbulk(t0_)
            if t0_ + 2 < NT:
                p1_head(t0_ + 2)
            p1_proj(t1_, "q", "wq", bqs)
            p1_pump()
            p1_qbulk(t1_)
            if t1_ + 2 < NT:
                p1_head(t1_ + 2)

        while dq_state["comp"] < len(dq_slices):
            dq_pump(1)
        emit("vector", lambda e: e.tensor_reduce(
            out=ksa[:, :], in_=kbuf[:, :].rearrange("p (c t) -> p c t", c=HC),
            axis=AX.X, op=ALU.add))
        emit("vector", lambda e: e.tensor_reduce(
            out=kva[:, :], in_=vbuf[:, :].rearrange("p (c t) -> p c t", c=HC),
            axis=AX.X, op=ALU.add), "V")
        v_kv_final = cnt["V"]
        tap("dkv", kva[:, :], "V", v_kv_final)
        tap("dks", ksa[:, :], "V", v_kv_final)
        gq_phase1 = qsnap("GQ")
        t_phase1 = cnt["T"]

        p2_loads = {}

        def p2_xq_load(t):
            wait_tok("sync", gq_phase1)
            wait("sync", "V", v_kv_final)
            wait("sync", "T", t_phase1)
            xfs = slot(0 if t % 2 == 0 else 7)
            qbs = slot(1 if t % 2 == 0 else 8)
            wait("sync", "V", p2_loads.get(t - 2, {}).get("xf_done", 0))
            wait("sync", "T", p2_loads.get(t - 2, {}).get("outT_mm", 0))
            d = p2_loads.setdefault(t, {})
            d["xf"] = dma("sync", "SQ", xfs,
                          bass.AP(xT_d, t * P * 2048, [[2048, P], [1, 2048]]))
            d["qb"] = dma("sync", "SQ", qbs,
                          bass.AP(qT_d, t * P * 2048, [[2048, P], [1, 2048]]))

        p2_specs = [("wo", sw_d["woh"], sw_d["wol"], 0, H)]
        for g in range(4):
            p2_specs.append((f"w1g{g}", w1h_d, w1l_d, g * P * 8192, FF))
        for g in range(4):
            p2_specs.append((f"w2g{g}", w2h_d, w2l_d, g * P * 8192, H))
        p2_wseq = [(t, i) for t in range(NT) for i in range(len(p2_specs))]
        p2_wcur = [0]

        def p2_pump():
            if p2_wcur[0] >= len(p2_wseq):
                return
            t, i = p2_wseq[p2_wcur[0]]
            p2_wcur[0] += 1
            key, hd, ld_, off, roww = p2_specs[i]
            sl = load_idx[0] % 2
            load_idx[0] += 1
            wait_tok("sync", gq_phase1)
            wait("sync", "T", ws_last_mm[sl])
            srch = bass.AP(hd, off, [[8192, P], [1, 8192]])
            srcl = bass.AP(ld_, off, [[8192, P], [1, 8192]])
            dma("sync", "SQ", ws[sl][:, 0:8192], srch)
            d = p2_loads.setdefault(t, {})
            d[key] = dma("sync", "SQ", ws[sl][:, 8192:16384], srcl)
            d[key + "_slot"] = sl

        p2_xq_load(0)
        p2_pump()
        p2_pump()

        def ln_stats(src_ap, sq_out, eps_col, extra=None):
            sq = emit("scalar", lambda e: e.activation(
                out=sq_out, in_=src_ap, func=AF.Square), "S")
            use_bank(2)
            c1 = None
            for c in range(HC):
                c1 = emit("tensor", lambda e, c=c: e.matmul(
                    pb[2][:, 0:T], ones[:, :], src_ap[:, c * T:(c + 1) * T],
                    start=(c == 0), stop=(c == HC - 1)), "T" if c == HC - 1 else None)
            wait("tensor", "S", sq)
            use_bank(3)
            c2 = None
            for c in range(HC):
                c2 = emit("tensor", lambda e, c=c: e.matmul(
                    pb[3][:, 0:T], ones[:, :], sq_out[:, c * T:(c + 1) * T],
                    start=(c == 0), stop=(c == HC - 1)), "T" if c == HC - 1 else None)
            ln_finish(c1, c2, eps_col, extra)

        def ln_finish(c1, c2, eps_col, extra=None):
            wait("vector", "T", c1)
            v1 = emit("vector", lambda e: e.tensor_scalar(
                out=mus[:, :], in0=pb[2][:, 0:T], scalar1=1.0 / H, scalar2=None,
                op0=ALU.mult), "V")
            bank_free[2] = ("V", v1)
            emit("vector", lambda e: e.tensor_tensor(
                out=msq[:, :], in0=mus[:, :], in1=mus[:, :], op=ALU.mult))
            wait("vector", "T", c2)
            v2 = emit("vector", lambda e: e.scalar_tensor_tensor(
                out=sds[:, :], in0=pb[3][:, 0:T], scalar=1.0 / H, in1=msq[:, :],
                op0=ALU.mult, op1=ALU.subtract), "V")
            bank_free[3] = ("V", v2)
            if extra == "ln2":
                v2 = emit("vector", lambda e: e.tensor_scalar(
                    out=tm2[:, :], in0=sds[:, :], scalar1=1.0 + EPS,
                    scalar2=EPS * EPS, op0=ALU.mult, op1=ALU.add), "V")
            wait("scalar", "V", v2)
            s2 = emit("scalar", lambda e, eps_col=eps_col: e.activation(
                out=tm3[:, :], in_=sds[:, :], func=AF.Sqrt,
                bias=cn[:, eps_col:eps_col + 1]), "S")
            if extra == "ln2":
                s2 = emit("scalar", lambda e: e.activation(
                    out=msq[:, :], in_=tm2[:, :], func=AF.Sqrt), "S")
            wait("vector", "S", s2)
            emit("vector", lambda e: e.reciprocal(out=rss[:, :], in_=tm3[:, :]))
            if extra == "ln2":
                emit("vector", lambda e: e.reciprocal(out=sds[:, :], in_=msq[:, :]))

        def p2_vstage(t):
            qb = slot(1 if t % 2 == 0 else 8)
            sc0, sc1, sc2 = slot(4), slot(5), slot(6)
            wait_tok("vector", p2_loads[t]["qb"])
            wait("vector", "V", v_kv_final)
            emit("vector", lambda e, qb=qb: e.tensor_tensor(
                out=r3(sc0, HC), in0=r3(qb, HC),
                in1=bass.AP(ksa, 0, [[HC, P], [1, HC], [0, T]]), op=ALU.mult))
            emit("vector", lambda e: e.tensor_scalar(
                out=sc1, in0=sc0, scalar1=1e-6, scalar2=None, op0=ALU.add))
            emit("vector", lambda e: e.reciprocal(out=sc0, in_=sc1))
            emit("vector", lambda e, qb=qb: e.tensor_tensor(
                out=r3(sc1, HC), in0=r3(qb, HC),
                in1=bass.AP(kva, 0, [[HC, P], [1, HC], [0, T]]), op=ALU.mult))
            emit("vector", lambda e: e.tensor_tensor(
                out=sc2, in0=sc1, in1=sc0, op=ALU.mult))
            wait("vector", "T", cnt["T"])
            emit("vector", lambda e: e.tensor_copy(out=b0[:, :], in_=sc2))
            p2_loads[t]["vsplit"] = emit("vector", lambda e: e.tensor_tensor(
                out=b1t[:, :], in0=sc2, in1=b0[:, :], op=ALU.subtract), "V")
            if t == 0 and debug:
                tap("dvv", sc2, "V", p2_loads[t]["vsplit"])

        def p2_wo(t):
            L = p2_loads[t]
            sc0 = slot(4)
            rn = slot(2 if t % 2 == 0 else 3)
            xf = slot(0 if t % 2 == 0 else 7)
            sl = L["wo_slot"]
            wh, wl = ws[sl][:, 0:8192], ws[sl][:, 8192:16384]
            wait_tok("tensor", L["wo"])
            wait("tensor", "V", L["vsplit"])
            if t == 0:
                wait("scalar", "V", cnt["V"])
            r1_tok = [None] * HC
            sq_tok = [None] * HC
            sqmm_cnt = [None] * HC
            cc = {}

            def emit_stats(c):
                if c == 0:
                    use_bank(2)
                    use_bank(3)
                wait("tensor", "V", r1_tok[c])
                r_ = emit("tensor", lambda e, c=c, rn=rn: e.matmul(
                    pb[2][:, 0:T], ones[:, :], rn[:, c * T:(c + 1) * T],
                    start=(c == 0), stop=(c == HC - 1)),
                    "T" if c == HC - 1 else None)
                if c == HC - 1:
                    cc["c1"] = r_
                wait("tensor", "S", sq_tok[c])
                r_ = emit("tensor", lambda e, c=c: e.matmul(
                    pb[3][:, 0:T], ones[:, :], sq2[:, (c % 2) * T:(c % 2 + 1) * T],
                    start=(c == 0), stop=(c == HC - 1)),
                    "T" if c == HC - 1 else None)
                if c == HC - 1:
                    cc["c2"] = r_
                sqmm_cnt[c] = cnt["T"]

            s_gelu = None
            for mc in range(HC):
                bkn = mc % 2
                use_bank(bkn)
                last = None
                for kc in range(HC):
                    for pi, (wop, xop) in enumerate(((wh, b0), (wh, b1t), (wl, b0))):
                        st = (kc == 0 and pi == 0)
                        sp = (kc == HC - 1 and pi == 2)
                        last = emit("tensor", lambda e, wop=wop, xop=xop, kc=kc, mc=mc, bkn=bkn, st=st, sp=sp: e.matmul(
                            pb[bkn][:, 0:T],
                            wop[:, kc * H + mc * P: kc * H + (mc + 1) * P],
                            xop[:, kc * T:(kc + 1) * T], start=st, stop=sp),
                            "T" if sp else None)
                ws_last_mm[sl] = cnt["T"]
                wait("scalar", "T", last)
                s_gelu = emit("scalar", lambda e, mc=mc, bkn=bkn: e.activation(
                    out=sc0[:, mc * T:(mc + 1) * T], in_=pb[bkn][:, 0:T],
                    func=AF.Gelu_apprx_tanh, bias=bos[:, mc:mc + 1]), "S")
                bank_free[bkn] = ("S", s_gelu)
                if mc == 0:
                    wait_tok("vector", L["xf"])
                wait("vector", "S", s_gelu)
                r1_tok[mc] = emit("vector", lambda e, mc=mc, xf=xf, rn=rn: e.tensor_tensor(
                    out=rn[:, mc * T:(mc + 1) * T], in0=sc0[:, mc * T:(mc + 1) * T],
                    in1=xf[:, mc * T:(mc + 1) * T], op=ALU.add), "V")
                if mc >= 2:
                    wait("scalar", "T", sqmm_cnt[mc - 2])
                wait("scalar", "V", r1_tok[mc])
                sq_tok[mc] = emit("scalar", lambda e, mc=mc, rn=rn: e.activation(
                    out=sq2[:, (mc % 2) * T:(mc % 2 + 1) * T],
                    in_=rn[:, mc * T:(mc + 1) * T], func=AF.Square), "S")
                if mc >= 1:
                    emit_stats(mc - 1)
            emit_stats(HC - 1)
            p2_pump()
            L["wo_last"] = cnt["T"]
            L["gelu"] = s_gelu
            L["xf_done"] = r1_tok[HC - 1]
            L["c1"] = cc["c1"]
            L["c2"] = cc["c2"]
            if t == 0:
                tap("dr1", rn, "V", r1_tok[HC - 1])

        def p2_out(t):
            r1 = slot(0 if t % 2 == 0 else 7)
            sc1 = slot(5)
            wait("tensor", "V", p2_loads[t]["v_ofm"])
            v_otm = None
            for fc in range(HC):
                for s_ in range(2):
                    bkn = 4 + (2 * fc + s_) % 4
                    use_bank(bkn)
                    c = emit("tensor", lambda e, fc=fc, s_=s_, bkn=bkn, r1=r1: e.transpose(
                        pb[bkn][:, 0:P],
                        r1[:, fc * T + s_ * P: fc * T + s_ * P + P],
                        ident[:, :]), "T")
                    wait("vector", "T", c)
                    v_otm = emit("vector", lambda e, fc=fc, s_=s_, bkn=bkn: e.tensor_copy(
                        out=sc1[:, s_ * H + fc * P: s_ * H + (fc + 1) * P],
                        in_=pb[bkn][:, 0:P]), "V")
                    bank_free[bkn] = ("V", v_otm)
            p2_loads[t]["outT_mm"] = cnt["T"]
            wait("gpsimd", "V", v_otm)
            p2_loads[t]["out_store"] = dma(
                "gpsimd", "GQ",
                bass.AP(out_d, t * T * H, [[H, P], [P * H, 2], [1, H]]), r3(sc1, 2))

        p2_vstage(0)
        p2_wo(0)

        for t in range(NT):
            L = p2_loads[t]
            xf = slot(0 if t % 2 == 0 else 7)
            qb = slot(1 if t % 2 == 0 else 8)
            r1 = slot(2 if t % 2 == 0 else 3)
            x1 = slot(3 if t % 2 == 0 else 2)
            sc0, sc1, sc2 = slot(4), slot(5), slot(6)
            prev = p2_loads.get(t - 1, {})

            wait_tok("scalar", p2_loads.get(t - 2, {}).get("out_store"))
            ln_finish(L["c1"], L["c2"], 0, extra="ln2")
            if t > 0:
                p2_out(t - 1)
            if t + 1 < NT:
                p2_xq_load(t + 1)
            wait_tok("vector", prev.get("out_store"))
            emit("vector", lambda e, r1=r1: e.tensor_tensor(
                out=r3(sc1, HC), in0=r3(r1, HC), in1=bc_chunk(mus), op=ALU.subtract))
            v_x1 = emit("vector", lambda e, x1=x1: e.tensor_tensor(
                out=r3(x1, HC), in0=r3(sc1, HC), in1=bc_chunk(rss), op=ALU.mult), "V")
            if t == 0 and debug:
                tap("dx1", x1, "V", v_x1)
            emit("vector", lambda e: e.tensor_tensor(
                out=r3(sc2, HC), in0=r3(sc1, HC), in1=bc_chunk(sds), op=ALU.mult))
            for c in range(HC):
                emit("vector", lambda e, c=c: e.tensor_scalar(
                    out=sc1[:, c * T:(c + 1) * T], in0=sc2[:, c * T:(c + 1) * T],
                    scalar1=gms[:, c:c + 1], scalar2=bts[:, c:c + 1],
                    op0=ALU.mult, op1=ALU.add))
            wait("vector", "T", cnt["T"])
            emit("vector", lambda e: e.tensor_copy(out=b0[:, :], in_=sc1))
            v_hsplit = emit("vector", lambda e: e.tensor_tensor(
                out=b1t[:, :], in0=sc1, in1=b0[:, :], op=ALU.subtract), "V")
            if t == 0 and debug:
                tap("dh", sc1, "V", v_hsplit)

            wait("tensor", "V", v_hsplit)
            v_ysplit = None
            v_yq = {}
            for fg in range(4):
                key = f"w1g{fg}"
                sl = L[key + "_slot"]
                wh, wl = ws[sl][:, 0:8192], ws[sl][:, 8192:16384]
                wait_tok("tensor", L[key])
                s_silu = None
                for mc in range(HC):
                    m = fg * 8 + mc
                    bkn = mc % 2
                    use_bank(bkn)
                    last = None
                    for kc in range(HC):
                        for pi, (wop, xop) in enumerate(((wh, b0), (wh, b1t), (wl, b0))):
                            st = (kc == 0 and pi == 0)
                            sp = (kc == HC - 1 and pi == 2)
                            last = emit("tensor", lambda e, wop=wop, xop=xop, kc=kc, mc=mc, bkn=bkn, st=st, sp=sp: e.matmul(
                                pb[bkn][:, 0:T],
                                wop[:, kc * H + mc * P: kc * H + (mc + 1) * P],
                                xop[:, kc * T:(kc + 1) * T], start=st, stop=sp),
                                "T" if sp else None)
                    ws_last_mm[sl] = cnt["T"]
                    wait("scalar", "T", last)
                    if v_ysplit is not None:
                        wait("scalar", "V", v_ysplit)
                    s_silu = emit("scalar", lambda e, mc=mc, bkn=bkn, m=m: e.activation(
                        out=sc0[:, mc * T:(mc + 1) * T], in_=pb[bkn][:, 0:T],
                        func=AF.Silu, bias=b1s[:, m:m + 1]), "S")
                    bank_free[bkn] = ("S", s_silu)
                p2_pump()
                wait("vector", "S", s_silu)
                wait_tok("vector", dq_state["laststore"])
                emit("vector", lambda e, fg=fg: e.tensor_copy(
                    out=y0[:, fg * 2048:(fg + 1) * 2048], in_=sc0))
                v_ysplit = emit("vector", lambda e, fg=fg: e.tensor_tensor(
                    out=y1[:, fg * 2048:(fg + 1) * 2048], in0=sc0,
                    in1=y0[:, fg * 2048:(fg + 1) * 2048], op=ALU.subtract), "V")
                v_yq[fg] = v_ysplit
                if t == 0 and fg == 0:
                    tap("dy", sc0, "V", v_ysplit)

            if t + 1 < NT:
                p2_vstage(t + 1)

            fc2_last = [None] * HC
            for kg in range(4):
                key = f"w2g{kg}"
                sl = L[key + "_slot"]
                wh, wl = ws[sl][:, 0:8192], ws[sl][:, 8192:16384]
                wait("tensor", "V", v_yq[kg])
                wait_tok("tensor", L[key])
                for mc in range(HC):
                    if kg == 0:
                        use_bank(mc)
                    for kc in range(HC):
                        fkc = kg * 8 + kc
                        for pi, (wop, yop) in enumerate(((wh, y0), (wh, y1), (wl, y0))):
                            st = (kg == 0 and kc == 0 and pi == 0)
                            sp = (kg == 3 and kc == HC - 1 and pi == 2)
                            sec_end = (mc == HC - 1 and kc == HC - 1 and pi == 2)
                            r_ = emit("tensor", lambda e, wop=wop, yop=yop, kc=kc, mc=mc, fkc=fkc, st=st, sp=sp: e.matmul(
                                pb[mc][:, 0:T],
                                wop[:, kc * H + mc * P: kc * H + (mc + 1) * P],
                                yop[:, fkc * T:(fkc + 1) * T], start=st, stop=sp),
                                "T" if (sp or sec_end) else None)
                            if sp:
                                fc2_last[mc] = r_ if r_ is not None else cnt["T"]
                ws_last_mm[sl] = cnt["T"]
                p2_pump()
            v_tp = None
            for mc in range(HC):
                wait("vector", "T", fc2_last[mc])
                emit("vector", lambda e, mc=mc: e.tensor_scalar(
                    out=mus[:, :], in0=pb[mc][:, 0:T], scalar1=0.25,
                    scalar2=b2q[:, mc:mc + 1], op0=ALU.mult, op1=ALU.add))
                v_tp = emit("vector", lambda e, mc=mc, x1=x1, r1=r1: e.tensor_tensor(
                    out=r1[:, mc * T:(mc + 1) * T], in0=mus[:, :],
                    in1=x1[:, mc * T:(mc + 1) * T], op=ALU.add), "V")
                bank_free[mc] = ("V", v_tp)
            if t == 0:
                tap("dtp", r1, "V", v_tp)

            wait("scalar", "V", v_tp)
            wait("tensor", "V", v_tp)
            ln_stats(r1, sc1, 1)
            emit("vector", lambda e, r1=r1: e.tensor_tensor(
                out=r3(sc2, HC), in0=r3(r1, HC), in1=bc_chunk(mus), op=ALU.subtract))
            v_ofm = emit("vector", lambda e, xf=xf: e.tensor_tensor(
                out=r3(xf, HC), in0=r3(sc2, HC), in1=bc_chunk(rss), op=ALU.mult), "V")

            if t + 1 < NT:
                p2_wo(t + 1)
            p2_loads[t]["v_ofm"] = v_ofm
            if t == NT - 1:
                p2_out(t)

        wait_tok("gpsimd", qsnap("GQ"))
        wait_tok("gpsimd", qsnap("SQ"))

        def replay(name):
            def run(eng):
                for item in ops[name]:
                    if item[0] == "wait":
                        eng.wait_ge(sems[item[1]], item[2])
                    else:
                        inst = item[1](eng)
                        if item[2]:
                            inst.then_inc(sems[item[2]], item[3])
            return run

        block.tensor(replay("tensor"))
        block.vector(replay("vector"))
        block.scalar(replay("scalar"))
        block.sync(replay("sync"))
        block.gpsimd(replay("gpsimd"))

    return nc


def kernel(L0_attn_x, L0_attn_Wq, L0_attn_bq, L0_attn_Wk, L0_attn_bk,
           L0_attn_Wv, L0_attn_bv, L0_attn_Wo, L0_attn_bo,
           L0_ffn_ln_gamma, L0_ffn_ln_beta, L0_ffn_fc1_q, L0_ffn_fc1_scales,
           L0_ffn_fc1_bias, L0_ffn_fc2_q, L0_ffn_fc2_scales, L0_ffn_fc2_bias):
    x = np.ascontiguousarray(np.asarray(L0_attn_x, dtype=np.float32))
    common = dict(
        wq=np.ascontiguousarray(np.asarray(L0_attn_Wq, np.float32)),
        wk=np.ascontiguousarray(np.asarray(L0_attn_Wk, np.float32)),
        wv=np.ascontiguousarray(np.asarray(L0_attn_Wv, np.float32)),
        wo=np.ascontiguousarray(np.asarray(L0_attn_Wo, np.float32)),
        bq=np.ascontiguousarray(np.asarray(L0_attn_bq, np.float32)),
        bk=np.ascontiguousarray(np.asarray(L0_attn_bk, np.float32)),
        bv=np.ascontiguousarray(np.asarray(L0_attn_bv, np.float32)),
        bo=np.ascontiguousarray(np.asarray(L0_attn_bo, np.float32)),
        gm=np.ascontiguousarray(np.asarray(L0_ffn_ln_gamma, np.float32)),
        bt=np.ascontiguousarray(np.asarray(L0_ffn_ln_beta, np.float32)),
        f1q=np.ascontiguousarray(np.asarray(L0_ffn_fc1_q, np.int32)),
        f1s=np.ascontiguousarray(np.asarray(L0_ffn_fc1_scales, np.float32)),
        f1b=np.ascontiguousarray(np.asarray(L0_ffn_fc1_bias, np.float32)),
        f2q=np.ascontiguousarray(np.asarray(L0_ffn_fc2_q, np.int32)),
        f2s=np.ascontiguousarray(np.asarray(L0_ffn_fc2_scales, np.float32)),
        f2b=np.ascontiguousarray(np.asarray(L0_ffn_fc2_bias, np.float32)),
    )
    in_maps = [dict(common, x=np.ascontiguousarray(x[c])) for c in range(8)]
    if _NC_CACHE[0] is None:
        _NC_CACHE[0] = build_nc()
    res = run_bass_kernel_spmd(_NC_CACHE[0], in_maps, core_ids=list(range(8)))
    return np.stack([res.results[c]["out"] for c in range(8)], axis=0)


# version 23
# speedup vs baseline: 1.0331x; 1.0331x over previous
import numpy as np
import concourse.bass as bass
import concourse.mybir as mybir
from concourse.bass_utils import run_bass_kernel_spmd
from contextlib import ExitStack

F32 = mybir.dt.float32
BF16 = mybir.dt.bfloat16
I32 = mybir.dt.int32
AF = mybir.ActivationFunctionType
ALU = mybir.AluOpType
AX = mybir.AxisListType

P = 128
H = 1024
HC = 8
S = 4096
T = 256
NT = S // T
FF = 4096
FC = 32
NSLOT = 2048
EPS = 1e-5

_NC_CACHE = [None]


def build_nc(debug=False):
    nc = bass.Bass("TRN2", target_bir_lowering=False, detect_race_conditions=False)

    x_d = nc.dram_tensor("x", [S, H], F32, kind="ExternalInput")
    wq_d = nc.dram_tensor("wq", [H, H], F32, kind="ExternalInput")
    wk_d = nc.dram_tensor("wk", [H, H], F32, kind="ExternalInput")
    wv_d = nc.dram_tensor("wv", [H, H], F32, kind="ExternalInput")
    wo_d = nc.dram_tensor("wo", [H, H], F32, kind="ExternalInput")
    bq_d = nc.dram_tensor("bq", [H], F32, kind="ExternalInput")
    bk_d = nc.dram_tensor("bk", [H], F32, kind="ExternalInput")
    bv_d = nc.dram_tensor("bv", [H], F32, kind="ExternalInput")
    bo_d = nc.dram_tensor("bo", [H], F32, kind="ExternalInput")
    gm_d = nc.dram_tensor("gm", [H], F32, kind="ExternalInput")
    bt_d = nc.dram_tensor("bt", [H], F32, kind="ExternalInput")
    f1q_d = nc.dram_tensor("f1q", [H, FF], I32, kind="ExternalInput")
    f1s_d = nc.dram_tensor("f1s", [HC, FF], F32, kind="ExternalInput")
    f1b_d = nc.dram_tensor("f1b", [FF], F32, kind="ExternalInput")
    f2q_d = nc.dram_tensor("f2q", [FF, H], I32, kind="ExternalInput")
    f2s_d = nc.dram_tensor("f2s", [FC, H], F32, kind="ExternalInput")
    f2b_d = nc.dram_tensor("f2b", [H], F32, kind="ExternalInput")
    out_d = nc.dram_tensor("out", [S, H], F32, kind="ExternalOutput")
    sck = "ExternalOutput" if debug else "Internal"
    xT_d = nc.dram_tensor("xT", [NT * P, 2048], F32, kind=sck)
    qT_d = nc.dram_tensor("qT", [NT * P, 2048], F32, kind=sck)
    taps = {}
    if debug:
        for nm, shape in (("dkv", [P, HC]), ("dks", [P, HC]), ("dvv", [P, 2048]),
                          ("dr1", [P, 2048]), ("dx1", [P, 2048]), ("dh", [P, 2048]),
                          ("dy", [P, 2048]), ("dtp", [P, 2048]),
                          ("dxm", [P, 2048]), ("dxf", [P, 2048]),
                          ("dkt", [P, 2048]), ("dvt", [P, 2048]),
                          ("dk15", [P, 2048]), ("dkp15", [P, HC]), ("dks15", [P, HC])):
            taps[nm] = nc.dram_tensor(nm, shape, F32, kind="ExternalOutput")
    sw_d = {}
    for nm in ("wq", "wk", "wv", "wo"):
        for hl in ("h", "l"):
            sw_d[nm + hl] = nc.dram_tensor("s" + nm + hl, [P, 8192], BF16, kind=sck)
    w1h_d = nc.dram_tensor("w1h", [HC * P, 8192], BF16, kind=sck)
    w1l_d = nc.dram_tensor("w1l", [HC * P, 8192], BF16, kind=sck)
    w2h_d = nc.dram_tensor("w2h", [4 * P, 8192], BF16, kind=sck)
    w2l_d = nc.dram_tensor("w2l", [4 * P, 8192], BF16, kind=sck)

    with ExitStack() as stack:
        sb = lambda name, shape, dt: stack.enter_context(nc.sbuf_tensor(name, shape, dt))
        ident = sb("ident", [P, P], F32)
        ones = sb("ones", [P, P], F32)
        cn = sb("cn", [P, 2], F32)
        bqs = sb("bqs", [P, HC], F32)
        bks = sb("bks", [P, HC], F32)
        bvs = sb("bvs", [P, HC], F32)
        bos = sb("bos", [P, HC], F32)
        gms = sb("gms", [P, HC], F32)
        bts = sb("bts", [P, HC], F32)
        b1s = sb("b1s", [P, FC], F32)
        b2s = sb("b2s", [P, HC], F32)
        b2q = sb("b2q", [P, HC], F32)
        kva = sb("kva", [P, HC], F32)
        ksa = sb("ksa", [P, HC], F32)
        mus = sb("mus", [P, T], F32)
        sds = sb("sds", [P, T], F32)
        msq = sb("msq", [P, T], F32)
        rss = sb("rss", [P, T], F32)
        tm2 = sb("tm2", [P, T], F32)
        tm3 = sb("tm3", [P, T], F32)
        kbuf = sb("kbuf", [P, HC * NT], F32)
        vbuf = sb("vbuf", [P, HC * NT], F32)
        dqb = sb("dqb", [P, 1536], F32)
        dsr = sb("dsr", [1, 1536], F32)
        dsc = sb("dsc", [P, 512], F32)
        apool = sb("apool", [P, 9 * NSLOT], F32)
        bh = [sb("bh0", [P, NSLOT], BF16), sb("bh1", [P, NSLOT], BF16)]
        bl = [sb("bl0", [P, NSLOT], BF16), sb("bl1", [P, NSLOT], BF16)]
        b0 = bh[0]
        b1t = bl[0]
        y0 = sb("y0", [P, FC * T], BF16)
        y1 = sb("y1", [P, FC * T], BF16)
        ws = [sb("ws0", [P, 16384], BF16), sb("ws1", [P, 16384], BF16)]
        pb = [stack.enter_context(nc.psum_tensor(f"pb{i}", [P, 512], F32)) for i in range(8)]
        semnames = (["T", "V", "S", "G"] + [f"SQ{i}" for i in range(8)]
                    + [f"GQ{i}" for i in range(4)])
        sems = {k: stack.enter_context(nc.semaphore("sem" + k)) for k in semnames}
        qpool = {"SQ": [f"SQ{i}" for i in range(8)], "GQ": [f"GQ{i}" for i in range(4)]}
        qeng = {"SQ": "sync", "GQ": "gpsimd"}
        qidx = {"SQ": 0, "GQ": 0}
        block = stack.enter_context(nc.Block())

        def slot(i, w=NSLOT):
            return apool[:, i * NSLOT:i * NSLOT + w]

        ops = {e: [] for e in ("tensor", "vector", "scalar", "sync", "gpsimd")}
        cnt = dict.fromkeys(sems, 0)
        last_wait = {}

        def emit(eng, fn, inc=None, n=1):
            ops[eng].append(("op", fn, inc, n))
            if inc:
                cnt[inc] += n
                return cnt[inc]
            return None

        def wait(eng, sem, val):
            if val is None or val <= 0:
                return
            key = (eng, sem)
            if last_wait.get(key, 0) >= val:
                return
            last_wait[key] = val
            ops[eng].append(("wait", sem, val))

        bank_free = [(None, 0)] * 8

        def use_bank(b):
            s, c = bank_free[b]
            if s:
                wait("tensor", s, c)

        ws_last_mm = [0, 0]
        load_idx = [0]

        def dma(eng, queue, out, in_, slow=False):
            sm = qpool[queue][qidx[queue] % len(qpool[queue])]
            qidx[queue] += 1
            wait(eng, sm, cnt[sm])
            val = emit(eng, (lambda o=out, i=in_, s=slow: (
                lambda e: e.dma_start(out=o, in_=i, allow_slow_non_contiguous=s)))(), sm, 16)
            return (sm, val)

        def wait_tok(eng, tok):
            if tok:
                if isinstance(tok, list):
                    for t_ in tok:
                        wait_tok(eng, t_)
                else:
                    wait(eng, tok[0], tok[1])

        def qsnap(queue):
            return [(sm, cnt[sm]) for sm in qpool[queue] if cnt[sm] > 0]

        def tap(nm, src_ap, sem, val):
            if debug and nm in taps:
                if isinstance(sem, tuple):
                    wait_tok("gpsimd", sem)
                else:
                    wait("gpsimd", sem, val)
                cols = taps[nm].shape[1]
                dma("gpsimd", "GQ", bass.AP(taps[nm], 0, [[cols, P], [1, cols]]), src_ap)

        def dr_tiles(handle, C, r0, nchunk, c0, width):
            return bass.AP(handle, r0 * C + c0, [[C, P], [P * C, nchunk], [1, width]])

        def bc_tok(handle, free):
            return bass.AP(handle, 0, [[free, P], [1, free], [0, T]])

        def bc_chunk(handle):
            return bass.AP(handle, 0, [[T, P], [0, HC], [1, T]])

        def r3(ap_, c):
            return ap_.rearrange("p (c t) -> p c t", c=c)

        g0 = emit("gpsimd", lambda e: e.memset(ident[:, :], 0.0), "G")
        wait("gpsimd", "G", g0)
        emit("gpsimd", lambda e: e.affine_select(
            out=ident[:, :], in_=ident[:, :], compare_op=ALU.not_equal,
            fill=1.0, base=0, pattern=[[-1, P]], channel_multiplier=1))
        emit("gpsimd", lambda e: e.memset(ones[:, :], 1.0))
        emit("gpsimd", lambda e: e.memset(cn[:, 0:1], EPS))
        g_consts = emit("gpsimd", lambda e: e.memset(cn[:, 1:2], EPS / 4.0), "G")

        for dst, src in ((bqs, bq_d), (bks, bk_d), (bvs, bv_d), (bos, bo_d),
                         (gms, gm_d), (bts, bt_d), (b2s, f2b_d)):
            dma("sync", "SQ", dst[:, :], bass.AP(src, 0, [[1, P], [P, HC]]), slow=True)
        dma("sync", "SQ", b1s[:, :], bass.AP(f1b_d, 0, [[1, P], [P, FC]]), slow=True)
        sq_params = qsnap("SQ")
        wait_tok("vector", sq_params)
        emit("vector", lambda e: e.tensor_scalar(out=b2q[:, :], in0=b2s[:, :],
                                                 scalar1=0.25, scalar2=None, op0=ALU.mult))
        v_init = cnt["V"]

        v_prev = {}
        gq_half_store = {}
        hidx = [0]

        def attn_split(wsrc, nm, half):
            i = hidx[0]
            hidx[0] += 1
            par = i % 2
            ybuf = y0 if par == 0 else y1
            yh = ybuf[:, 0:4096]
            yl = ybuf[:, 4096:8192]
            wait("sync", "V", v_prev.get(i - 2, 0))
            ld1 = dma("sync", "SQ", r3(slot(5 + par, 2048), 2),
                      dr_tiles(wsrc, H, half * 512, 2, 0, H))
            ld2 = dma("sync", "SQ", r3(slot(7 + par, 2048), 2),
                      dr_tiles(wsrc, H, half * 512 + 256, 2, 0, H))
            wait_tok("vector", ld1)
            wait_tok("vector", ld2)
            wait_tok("vector", gq_half_store.get(i - 2))
            emit("vector", lambda e, par=par, yh=yh: e.tensor_copy(
                out=yh[:, 0:2048], in_=slot(5 + par, 2048)))
            emit("vector", lambda e, par=par, yh=yh: e.tensor_copy(
                out=yh[:, 2048:4096], in_=slot(7 + par, 2048)))
            emit("vector", lambda e, par=par, yh=yh, yl=yl: e.tensor_tensor(
                out=yl[:, 0:2048], in0=slot(5 + par, 2048), in1=yh[:, 0:2048],
                op=ALU.subtract))
            vs = emit("vector", lambda e, par=par, yh=yh, yl=yl: e.tensor_tensor(
                out=yl[:, 2048:4096], in0=slot(7 + par, 2048), in1=yh[:, 2048:4096],
                op=ALU.subtract), "V")
            v_prev[i] = vs
            wait("gpsimd", "V", vs)
            t1 = dma("gpsimd", "GQ",
                     bass.AP(sw_d[nm + "h"], half * 4096, [[8192, P], [1, 4096]]), yh)
            t2 = dma("gpsimd", "GQ",
                     bass.AP(sw_d[nm + "l"], half * 4096, [[8192, P], [1, 4096]]), yl)
            gq_half_store[i] = t2
            sw_toks.setdefault(nm, []).extend([t1, t2])

        sw_toks = {}
        for wsrc, nm in ((wv_d, "wv"), (wk_d, "wk"), (wq_d, "wq"), (wo_d, "wo")):
            for half in range(2):
                attn_split(wsrc, nm, half)
        gq_prev_store = [gq_half_store[hidx[0] - 2], gq_half_store[hidx[0] - 1]]

        dq_slices = []
        for g in range(HC):
            for j in range(FF // 512):
                dq_slices.append((f1q_d, f1s_d, w1h_d, w1l_d, g, j, FF))
        for g in range(FC):
            for j in range(H // 512):
                dq_slices.append((f2q_d, f2s_d, w2h_d, w2l_d, g, j, H))
        dq_state = {"load": 0, "comp": 0, "v_par": {}, "t_par": {},
                    "g_store": gq_prev_store,
                    "lq": {}, "ls": {}, "done": False, "laststore": None}
        DQDEPTH = 3

        def dq_load(i):
            qd, sd, wh_d_, wl_d_, g, j, ncols = dq_slices[i]
            par = i % DQDEPTH
            wait("gpsimd", "T", dq_state["t_par"].get(i - DQDEPTH, 0))
            dq_state["ls"][i] = dma(
                "gpsimd", "GQ", dsr[0:1, par * 512:(par + 1) * 512],
                bass.AP(sd, g * ncols + j * 512, [[ncols, 1], [1, 512]]))
            wait("gpsimd", "V", dq_state["v_par"].get(i - DQDEPTH, 0))
            dq_state["lq"][i] = dma(
                "gpsimd", "GQ", dqb[:, par * 512:(par + 1) * 512],
                bass.AP(qd, g * P * ncols + j * 512, [[ncols, P], [1, 512]]))

        def dq_pump(n):
            while dq_state["load"] < min(dq_state["comp"] + DQDEPTH, len(dq_slices)):
                dq_load(dq_state["load"])
                dq_state["load"] += 1
            for _ in range(n):
                i = dq_state["comp"]
                if i >= len(dq_slices):
                    dq_state["done"] = True
                    return
                dq_state["comp"] = i + 1
                qd, sd, wh_d_, wl_d_, g, j, ncols = dq_slices[i]
                par = i % DQDEPTH
                nslc = ncols // 512
                bkn = 4 + i % 2
                use_bank(bkn)
                wait_tok("tensor", dq_state["ls"][i])
                wait("tensor", "G", g_consts)
                tcnt = emit("tensor", lambda e, par=par, bkn=bkn: e.matmul(
                    pb[bkn][:, 0:512], ones[0:1, :],
                    dsr[0:1, par * 512:(par + 1) * 512],
                    start=True, stop=True), "T")
                dq_state["t_par"][i] = tcnt
                wait("vector", "T", tcnt)
                wait_tok("vector", dq_state["lq"][i])
                if j == 0:
                    wait_tok("vector", dq_state["g_store"])
                emit("vector", lambda e, par=par, bkn=bkn: e.scalar_tensor_tensor(
                    out=dsc[:, :], in0=dqb[:, par * 512:(par + 1) * 512],
                    scalar=-128.0, in1=pb[bkn][:, 0:512],
                    op0=ALU.add, op1=ALU.mult))
                emit("vector", lambda e, j=j: e.tensor_copy(
                    out=y0[:, j * 512:(j + 1) * 512], in_=dsc[:, :]))
                vcnt = emit("vector", lambda e, j=j: e.tensor_tensor(
                    out=y0[:, 4096 + j * 512: 4096 + (j + 1) * 512], in0=dsc[:, :],
                    in1=y0[:, j * 512:(j + 1) * 512], op=ALU.subtract), "V")
                dq_state["v_par"][i] = vcnt
                bank_free[bkn] = ("V", vcnt)
                if j == nslc - 1:
                    wait("gpsimd", "V", vcnt)
                    if ncols == FF:
                        for mg in range(4):
                            dma("gpsimd", "GQ",
                                bass.AP(wh_d_, mg * P * 8192 + g * 1024,
                                        [[8192, P], [1, 1024]]),
                                y0[:, mg * 1024:(mg + 1) * 1024])
                            dq_state["g_store"] = dma(
                                "gpsimd", "GQ",
                                bass.AP(wl_d_, mg * P * 8192 + g * 1024,
                                        [[8192, P], [1, 1024]]),
                                y0[:, 4096 + mg * 1024:4096 + (mg + 1) * 1024])
                    else:
                        kg2, kc2 = g // 8, g % 8
                        dma("gpsimd", "GQ",
                            bass.AP(wh_d_, kg2 * P * 8192 + kc2 * 1024,
                                    [[8192, P], [1, 1024]]),
                            y0[:, 0:1024])
                        dq_state["g_store"] = dma(
                            "gpsimd", "GQ",
                            bass.AP(wl_d_, kg2 * P * 8192 + kc2 * 1024,
                                    [[8192, P], [1, 1024]]),
                            y0[:, 4096:4096 + 1024])
                    dq_state["laststore"] = dq_state["g_store"]

        gq_phase0 = qsnap("GQ")
        v_phase0 = cnt["V"]

        p1_loads = {}
        fm_prev = [0]

        def p1_x_load(t):
            xs = slot(t % 2)
            wait("sync", "T", p1_loads.get(t - 2, {}).get("x_mm_cnt", 0))
            p1_loads.setdefault(t, {})["x"] = dma(
                "sync", "SQ", r3(xs, 2),
                bass.AP(x_d, t * T * H, [[H, P], [P * H, 2], [1, H]]))

        p1_wseq = [(pr, nm) for pr in range(NT // 2) for nm in ("wv", "wk", "wq")]
        p1_wcur = [0]

        def p1_pump():
            if p1_wcur[0] >= len(p1_wseq):
                return
            pr, nm = p1_wseq[p1_wcur[0]]
            p1_wcur[0] += 1
            sl = load_idx[0] % 2
            load_idx[0] += 1
            wait_tok("sync", sw_toks[nm])
            wait("sync", "T", ws_last_mm[sl])
            dma("sync", "SQ", ws[sl][:, 0:8192],
                bass.AP(sw_d[nm + "h"], 0, [[8192, P], [1, 8192]]))
            d = p1_loads.setdefault(2 * pr, {})
            d[nm] = dma("sync", "SQ", ws[sl][:, 8192:16384],
                        bass.AP(sw_d[nm + "l"], 0, [[8192, P], [1, 8192]]))
            d[nm + "_slot"] = sl

        p1_x_load(0)
        p1_x_load(1)
        p1_pump()
        p1_pump()

        def p1_head(t):
            L = p1_loads[t]
            xs = slot(t % 2)
            xf = slot(2)
            if t == 0 and debug:
                tap("dxm", xs, L["x"], None)
            wait_tok("tensor", L["x"])
            wait("tensor", "G", g_consts)
            wait_tok("vector", p1_loads.get(t - 1, {}).get("xT"))
            for fc in range(HC):
                for s_ in range(2):
                    bkn = 6 + (2 * fc + s_) % 2
                    use_bank(bkn)
                    c = emit("tensor", lambda e, fc=fc, s_=s_, bkn=bkn, xs=xs: e.transpose(
                        pb[bkn][:, 0:P], xs[:, s_ * H + fc * P: s_ * H + (fc + 1) * P],
                        ident[:, :]), "T")
                    wait("vector", "T", c)
                    vc = emit("vector", lambda e, fc=fc, s_=s_, bkn=bkn, xf=xf: e.tensor_copy(
                        out=xf[:, fc * T + s_ * P: fc * T + s_ * P + P],
                        in_=pb[bkn][:, 0:P]), "V")
                    bank_free[bkn] = ("V", vc)
            L["x_mm_cnt"] = cnt["T"]
            v_xf = cnt["V"]
            if t == 0 and debug:
                tap("dxf", xf, "V", v_xf)
            wait("gpsimd", "V", v_xf)
            L["xT"] = dma("gpsimd", "GQ",
                          bass.AP(xT_d, t * P * 2048, [[2048, P], [1, 2048]]), xf)
            wait("vector", "T", cnt["T"])
            emit("vector", lambda e, xf=xf, t=t: e.tensor_copy(out=bh[t % 2][:, :], in_=xf))
            L["split"] = emit("vector", lambda e, xf=xf, t=t: e.tensor_tensor(
                out=bl[t % 2][:, :], in0=xf, in1=bh[t % 2][:, :], op=ALU.subtract), "V")

        p1_head(0)
        p1_head(1)

        def p1_proj(t, proj, nm, bias):
            L0 = p1_loads[2 * (t // 2)]
            L = p1_loads[t]
            bhx = bh[t % 2]
            blx = bl[t % 2]
            sl = L0[nm + "_slot"]
            wait_tok("tensor", L0[nm])
            wait("tensor", "V", L["split"])
            wh = ws[sl][:, 0:8192]
            wl = ws[sl][:, 8192:16384]
            for mc in range(HC):
                bkn = mc % 4
                use_bank(bkn)
                last = None
                for kc in range(HC):
                    for pi, (wop, xop) in enumerate(((wh, bhx), (wh, blx), (wl, bhx))):
                        st = (kc == 0 and pi == 0)
                        sp = (kc == HC - 1 and pi == 2)
                        last = emit("tensor", lambda e, wop=wop, xop=xop, kc=kc, mc=mc, bkn=bkn, st=st, sp=sp: e.matmul(
                            pb[bkn][:, 0:T],
                            wop[:, kc * H + mc * P: kc * H + (mc + 1) * P],
                            xop[:, kc * T:(kc + 1) * T], start=st, stop=sp),
                            "T" if sp else None)
                ws_last_mm[sl] = cnt["T"]
                dstslot = {"v": slot(5), "k": slot(3), "q": slot(4)}[proj]
                wait("vector", "T", last)
                vc = emit("vector", lambda e, mc=mc, bkn=bkn, dstslot=dstslot, bias=bias: e.tensor_scalar(
                    out=dstslot[:, mc * T:(mc + 1) * T], in0=pb[bkn][:, 0:T],
                    scalar1=bias[:, mc:mc + 1], scalar2=None, op0=ALU.add), "V")
                bank_free[bkn] = ("V", vc)
                L[proj + "_evac"] = vc
            dq_pump(3)

        def p1_kbulk(t):
            L = p1_loads[t]
            kb = slot(3)
            wait_tok("vector", p1_loads.get(t - 1, {}).get("qT"))
            wait("scalar", "V", fm_prev[0])
            wait("scalar", "V", L["k_evac"])
            sE = emit("scalar", lambda e: e.activation(
                out=slot(7), in_=kb, func=AF.Exp), "S")
            emit("vector", lambda e: e.tensor_scalar(
                out=slot(8), in0=kb, scalar1=0.0, scalar2=None, op0=ALU.max))
            wait("vector", "S", sE)
            fm_prev[0] = emit("vector", lambda e: e.scalar_tensor_tensor(
                out=kb, in0=slot(7), scalar=1.0, in1=slot(8),
                op0=ALU.min, op1=ALU.add), "V")
            emit("vector", lambda e, t=t: e.tensor_reduce(
                out=bass.AP(kbuf, t, [[HC * NT, P], [NT, HC]]),
                in_=r3(kb, HC), axis=AX.X, op=ALU.add))
            emit("vector", lambda e: e.tensor_tensor(
                out=slot(8), in0=kb, in1=slot(5), op=ALU.mult))
            vkv = emit("vector", lambda e, t=t: e.tensor_reduce(
                out=bass.AP(vbuf, t, [[HC * NT, P], [NT, HC]]),
                in_=r3(slot(8), HC), axis=AX.X, op=ALU.add), "V")
            if t == 0 and debug:
                tap("dkt", kb, "V", vkv)
                tap("dvt", slot(5), "V", vkv)

        def p1_qbulk(t):
            L = p1_loads[t]
            qb1 = slot(4)
            qdst = slot(6)
            wait_tok("vector", p1_loads.get(t - 1, {}).get("qT"))
            wait("scalar", "V", fm_prev[0])
            wait("scalar", "V", L["q_evac"])
            sE = emit("scalar", lambda e, qb1=qb1: e.activation(
                out=slot(7), in_=qb1, func=AF.Exp), "S")
            emit("vector", lambda e, qb1=qb1: e.tensor_scalar(
                out=slot(8), in0=qb1, scalar1=0.0, scalar2=None, op0=ALU.max))
            wait("vector", "S", sE)
            vq = emit("vector", lambda e, qdst=qdst: e.scalar_tensor_tensor(
                out=qdst, in0=slot(7), scalar=1.0, in1=slot(8),
                op0=ALU.min, op1=ALU.add), "V")
            fm_prev[0] = vq
            wait("gpsimd", "V", vq)
            L["qT"] = dma("gpsimd", "GQ",
                          bass.AP(qT_d, t * P * 2048, [[2048, P], [1, 2048]]),
                          qdst)

        for pr in range(NT // 2):
            t0_, t1_ = 2 * pr, 2 * pr + 1
            p1_proj(t0_, "v", "wv", bvs)
            p1_proj(t0_, "k", "wk", bks)
            p1_kbulk(t0_)
            if t0_ + 2 < NT:
                p1_x_load(t0_ + 2)
            p1_proj(t1_, "v", "wv", bvs)
            p1_pump()
            p1_proj(t1_, "k", "wk", bks)
            p1_pump()
            p1_kbulk(t1_)
            if t1_ + 2 < NT:
                p1_x_load(t1_ + 2)
            p1_proj(t0_, "q", "wq", bqs)
            p1_qbulk(t0_)
            if t0_ + 2 < NT:
                p1_head(t0_ + 2)
            p1_proj(t1_, "q", "wq", bqs)
            p1_pump()
            p1_qbulk(t1_)
            if t1_ + 2 < NT:
                p1_head(t1_ + 2)

        while dq_state["comp"] < len(dq_slices):
            dq_pump(1)
        emit("vector", lambda e: e.tensor_reduce(
            out=ksa[:, :], in_=kbuf[:, :].rearrange("p (c t) -> p c t", c=HC),
            axis=AX.X, op=ALU.add))
        emit("vector", lambda e: e.tensor_reduce(
            out=kva[:, :], in_=vbuf[:, :].rearrange("p (c t) -> p c t", c=HC),
            axis=AX.X, op=ALU.add), "V")
        v_kv_final = cnt["V"]
        tap("dkv", kva[:, :], "V", v_kv_final)
        tap("dks", ksa[:, :], "V", v_kv_final)
        gq_phase1 = qsnap("GQ")
        t_phase1 = cnt["T"]

        p2_loads = {}

        def p2_xq_load(t):
            wait_tok("sync", gq_phase1)
            wait("sync", "V", v_kv_final)
            wait("sync", "T", t_phase1)
            xfs = slot(0 if t % 2 == 0 else 7)
            qbs = slot(1 if t % 2 == 0 else 8)
            wait("sync", "V", p2_loads.get(t - 2, {}).get("xf_done", 0))
            wait("sync", "T", p2_loads.get(t - 2, {}).get("outT_mm", 0))
            d = p2_loads.setdefault(t, {})
            d["xf"] = dma("sync", "SQ", xfs,
                          bass.AP(xT_d, t * P * 2048, [[2048, P], [1, 2048]]))
            d["qb"] = dma("sync", "SQ", qbs,
                          bass.AP(qT_d, t * P * 2048, [[2048, P], [1, 2048]]))

        p2_specs = [("wo", sw_d["woh"], sw_d["wol"], 0, H)]
        for g in range(4):
            p2_specs.append((f"w1g{g}", w1h_d, w1l_d, g * P * 8192, FF))
        for g in range(4):
            p2_specs.append((f"w2g{g}", w2h_d, w2l_d, g * P * 8192, H))
        p2_wseq = [(t, i) for t in range(NT) for i in range(len(p2_specs))]
        p2_wcur = [0]
        last_ln3_c2 = [0]

        def p2_pump():
            if p2_wcur[0] >= len(p2_wseq):
                return
            t, i = p2_wseq[p2_wcur[0]]
            p2_wcur[0] += 1
            key, hd, ld_, off, roww = p2_specs[i]
            sl = load_idx[0] % 2
            load_idx[0] += 1
            wait_tok("sync", gq_phase1)
            wait("sync", "T", ws_last_mm[sl])
            srch = bass.AP(hd, off, [[8192, P], [1, 8192]])
            srcl = bass.AP(ld_, off, [[8192, P], [1, 8192]])
            dma("sync", "SQ", ws[sl][:, 0:8192], srch)
            d = p2_loads.setdefault(t, {})
            d[key] = dma("sync", "SQ", ws[sl][:, 8192:16384], srcl)
            d[key + "_slot"] = sl

        p2_xq_load(0)
        p2_pump()
        p2_pump()

        def ln_stats(src_ap, sq_out, eps_col, extra=None):
            sq = emit("scalar", lambda e: e.activation(
                out=sq_out, in_=src_ap, func=AF.Square), "S")
            use_bank(2)
            c1 = None
            for c in range(HC):
                c1 = emit("tensor", lambda e, c=c: e.matmul(
                    pb[2][:, 0:T], ones[:, :], src_ap[:, c * T:(c + 1) * T],
                    start=(c == 0), stop=(c == HC - 1)), "T" if c == HC - 1 else None)
            wait("tensor", "S", sq)
            use_bank(3)
            c2 = None
            for c in range(HC):
                c2 = emit("tensor", lambda e, c=c: e.matmul(
                    pb[3][:, 0:T], ones[:, :], sq_out[:, c * T:(c + 1) * T],
                    start=(c == 0), stop=(c == HC - 1)), "T" if c == HC - 1 else None)
            ln_finish(c1, c2, eps_col, extra)
            return c2

        def ln_finish(c1, c2, eps_col, extra=None):
            wait("vector", "T", c1)
            v1 = emit("vector", lambda e: e.tensor_scalar(
                out=mus[:, :], in0=pb[2][:, 0:T], scalar1=1.0 / H, scalar2=None,
                op0=ALU.mult), "V")
            bank_free[2] = ("V", v1)
            emit("vector", lambda e: e.tensor_tensor(
                out=msq[:, :], in0=mus[:, :], in1=mus[:, :], op=ALU.mult))
            wait("vector", "T", c2)
            v2 = emit("vector", lambda e: e.scalar_tensor_tensor(
                out=sds[:, :], in0=pb[3][:, 0:T], scalar=1.0 / H, in1=msq[:, :],
                op0=ALU.mult, op1=ALU.subtract), "V")
            bank_free[3] = ("V", v2)
            if extra == "ln2":
                v2 = emit("vector", lambda e: e.tensor_scalar(
                    out=tm2[:, :], in0=sds[:, :], scalar1=1.0 + EPS,
                    scalar2=EPS * EPS, op0=ALU.mult, op1=ALU.add), "V")
            wait("scalar", "V", v2)
            s2 = emit("scalar", lambda e, eps_col=eps_col: e.activation(
                out=tm3[:, :], in_=sds[:, :], func=AF.Sqrt,
                bias=cn[:, eps_col:eps_col + 1]), "S")
            if extra == "ln2":
                s2 = emit("scalar", lambda e: e.activation(
                    out=msq[:, :], in_=tm2[:, :], func=AF.Sqrt), "S")
            wait("vector", "S", s2)
            emit("vector", lambda e: e.reciprocal(out=rss[:, :], in_=tm3[:, :]))
            if extra == "ln2":
                emit("vector", lambda e: e.reciprocal(out=sds[:, :], in_=msq[:, :]))

        def p2_vstage(t):
            qb = slot(1 if t % 2 == 0 else 8)
            sc0, sc1, sc2 = slot(4), slot(5), slot(6)
            wait_tok("vector", p2_loads[t]["qb"])
            wait("vector", "V", v_kv_final)
            emit("vector", lambda e, qb=qb: e.tensor_tensor(
                out=r3(sc0, HC), in0=r3(qb, HC),
                in1=bass.AP(ksa, 0, [[HC, P], [1, HC], [0, T]]), op=ALU.mult))
            emit("vector", lambda e: e.tensor_scalar(
                out=sc1, in0=sc0, scalar1=1e-6, scalar2=None, op0=ALU.add))
            emit("vector", lambda e: e.reciprocal(out=sc0, in_=sc1))
            emit("vector", lambda e, qb=qb: e.tensor_tensor(
                out=r3(sc1, HC), in0=r3(qb, HC),
                in1=bass.AP(kva, 0, [[HC, P], [1, HC], [0, T]]), op=ALU.mult))
            emit("vector", lambda e: e.tensor_tensor(
                out=sc2, in0=sc1, in1=sc0, op=ALU.mult))
            wait("vector", "T", cnt["T"])
            emit("vector", lambda e: e.tensor_copy(out=b0[:, :], in_=sc2))
            p2_loads[t]["vsplit"] = emit("vector", lambda e: e.tensor_tensor(
                out=b1t[:, :], in0=sc2, in1=b0[:, :], op=ALU.subtract), "V")
            if t == 0 and debug:
                tap("dvv", sc2, "V", p2_loads[t]["vsplit"])

        def p2_wo(t):
            L = p2_loads[t]
            sc0 = slot(4)
            rn = slot(2 if t % 2 == 0 else 3)
            xf = slot(0 if t % 2 == 0 else 7)
            sl = L["wo_slot"]
            wh, wl = ws[sl][:, 0:8192], ws[sl][:, 8192:16384]
            sqb = slot(5)
            wait_tok("tensor", L["wo"])
            wait("tensor", "V", L["vsplit"])
            if t == 0:
                wait("scalar", "V", cnt["V"])
                wait("scalar", "T", cnt["T"])
            else:
                wait("scalar", "T", last_ln3_c2[0])
            r1_tok = [None] * HC
            sq_tok = [None] * HC
            s_gelu = None
            for mc in range(HC):
                bkn = mc % 2
                use_bank(bkn)
                last = None
                for kc in range(HC):
                    for pi, (wop, xop) in enumerate(((wh, b0), (wh, b1t), (wl, b0))):
                        st = (kc == 0 and pi == 0)
                        sp = (kc == HC - 1 and pi == 2)
                        last = emit("tensor", lambda e, wop=wop, xop=xop, kc=kc, mc=mc, bkn=bkn, st=st, sp=sp: e.matmul(
                            pb[bkn][:, 0:T],
                            wop[:, kc * H + mc * P: kc * H + (mc + 1) * P],
                            xop[:, kc * T:(kc + 1) * T], start=st, stop=sp),
                            "T" if sp else None)
                ws_last_mm[sl] = cnt["T"]
                wait("scalar", "T", last)
                s_gelu = emit("scalar", lambda e, mc=mc, bkn=bkn: e.activation(
                    out=sc0[:, mc * T:(mc + 1) * T], in_=pb[bkn][:, 0:T],
                    func=AF.Gelu_apprx_tanh, bias=bos[:, mc:mc + 1]), "S")
                bank_free[bkn] = ("S", s_gelu)
                if mc == 0:
                    wait_tok("vector", L["xf"])
                wait("vector", "S", s_gelu)
                r1_tok[mc] = emit("vector", lambda e, mc=mc, xf=xf, rn=rn: e.tensor_tensor(
                    out=rn[:, mc * T:(mc + 1) * T], in0=sc0[:, mc * T:(mc + 1) * T],
                    in1=xf[:, mc * T:(mc + 1) * T], op=ALU.add), "V")
                wait("scalar", "V", r1_tok[mc])
                sq_tok[mc] = emit("scalar", lambda e, mc=mc, rn=rn, sqb=sqb: e.activation(
                    out=sqb[:, mc * T:(mc + 1) * T],
                    in_=rn[:, mc * T:(mc + 1) * T], func=AF.Square), "S")
            use_bank(2)
            c1 = None
            for c in range(HC):
                wait("tensor", "V", r1_tok[c])
                c1 = emit("tensor", lambda e, c=c, rn=rn: e.matmul(
                    pb[2][:, 0:T], ones[:, :], rn[:, c * T:(c + 1) * T],
                    start=(c == 0), stop=(c == HC - 1)),
                    "T" if c == HC - 1 else None)
            use_bank(3)
            c2 = None
            for c in range(HC):
                wait("tensor", "S", sq_tok[c])
                c2 = emit("tensor", lambda e, c=c, sqb=sqb: e.matmul(
                    pb[3][:, 0:T], ones[:, :], sqb[:, c * T:(c + 1) * T],
                    start=(c == 0), stop=(c == HC - 1)),
                    "T" if c == HC - 1 else None)
            p2_pump()
            L["wo_last"] = cnt["T"]
            L["gelu"] = s_gelu
            L["xf_done"] = r1_tok[HC - 1]
            L["c1"] = c1
            L["c2"] = c2
            if t == 0:
                tap("dr1", rn, "V", r1_tok[HC - 1])

        def p2_out(t):
            r1 = slot(0 if t % 2 == 0 else 7)
            sc1 = slot(5)
            wait("tensor", "V", p2_loads[t]["v_ofm"])
            v_otm = None
            for fc in range(HC):
                for s_ in range(2):
                    bkn = 4 + (2 * fc + s_) % 4
                    use_bank(bkn)
                    c = emit("tensor", lambda e, fc=fc, s_=s_, bkn=bkn, r1=r1: e.transpose(
                        pb[bkn][:, 0:P],
                        r1[:, fc * T + s_ * P: fc * T + s_ * P + P],
                        ident[:, :]), "T")
                    wait("vector", "T", c)
                    v_otm = emit("vector", lambda e, fc=fc, s_=s_, bkn=bkn: e.tensor_copy(
                        out=sc1[:, s_ * H + fc * P: s_ * H + (fc + 1) * P],
                        in_=pb[bkn][:, 0:P]), "V")
                    bank_free[bkn] = ("V", v_otm)
            p2_loads[t]["outT_mm"] = cnt["T"]
            wait("gpsimd", "V", v_otm)
            p2_loads[t]["out_store"] = dma(
                "gpsimd", "GQ",
                bass.AP(out_d, t * T * H, [[H, P], [P * H, 2], [1, H]]), r3(sc1, 2))

        p2_vstage(0)
        p2_wo(0)

        for t in range(NT):
            L = p2_loads[t]
            xf = slot(0 if t % 2 == 0 else 7)
            qb = slot(1 if t % 2 == 0 else 8)
            r1 = slot(2 if t % 2 == 0 else 3)
            x1 = slot(3 if t % 2 == 0 else 2)
            sc0, sc1, sc2 = slot(4), slot(5), slot(6)
            prev = p2_loads.get(t - 1, {})

            wait_tok("scalar", p2_loads.get(t - 2, {}).get("out_store"))
            ln_finish(L["c1"], L["c2"], 0, extra="ln2")
            if t > 0:
                p2_out(t - 1)
            if t + 1 < NT:
                p2_xq_load(t + 1)
            wait_tok("vector", prev.get("out_store"))
            emit("vector", lambda e, r1=r1: e.tensor_tensor(
                out=r3(sc1, HC), in0=r3(r1, HC), in1=bc_chunk(mus), op=ALU.subtract))
            v_x1 = emit("vector", lambda e, x1=x1: e.tensor_tensor(
                out=r3(x1, HC), in0=r3(sc1, HC), in1=bc_chunk(rss), op=ALU.mult), "V")
            if t == 0 and debug:
                tap("dx1", x1, "V", v_x1)
            emit("vector", lambda e: e.tensor_tensor(
                out=r3(sc2, HC), in0=r3(sc1, HC), in1=bc_chunk(sds), op=ALU.mult))
            for c in range(HC):
                emit("vector", lambda e, c=c: e.tensor_scalar(
                    out=sc1[:, c * T:(c + 1) * T], in0=sc2[:, c * T:(c + 1) * T],
                    scalar1=gms[:, c:c + 1], scalar2=bts[:, c:c + 1],
                    op0=ALU.mult, op1=ALU.add))
            wait("vector", "T", cnt["T"])
            emit("vector", lambda e: e.tensor_copy(out=b0[:, :], in_=sc1))
            v_hsplit = emit("vector", lambda e: e.tensor_tensor(
                out=b1t[:, :], in0=sc1, in1=b0[:, :], op=ALU.subtract), "V")
            if t == 0 and debug:
                tap("dh", sc1, "V", v_hsplit)

            wait("tensor", "V", v_hsplit)
            v_ysplit = None
            v_yq = {}
            for fg in range(4):
                key = f"w1g{fg}"
                sl = L[key + "_slot"]
                wh, wl = ws[sl][:, 0:8192], ws[sl][:, 8192:16384]
                wait_tok("tensor", L[key])
                s_silu = None
                for mc in range(HC):
                    m = fg * 8 + mc
                    bkn = mc % 2
                    use_bank(bkn)
                    last = None
                    for kc in range(HC):
                        for pi, (wop, xop) in enumerate(((wh, b0), (wh, b1t), (wl, b0))):
                            st = (kc == 0 and pi == 0)
                            sp = (kc == HC - 1 and pi == 2)
                            last = emit("tensor", lambda e, wop=wop, xop=xop, kc=kc, mc=mc, bkn=bkn, st=st, sp=sp: e.matmul(
                                pb[bkn][:, 0:T],
                                wop[:, kc * H + mc * P: kc * H + (mc + 1) * P],
                                xop[:, kc * T:(kc + 1) * T], start=st, stop=sp),
                                "T" if sp else None)
                    ws_last_mm[sl] = cnt["T"]
                    wait("scalar", "T", last)
                    if v_ysplit is not None:
                        wait("scalar", "V", v_ysplit)
                    s_silu = emit("scalar", lambda e, mc=mc, bkn=bkn, m=m: e.activation(
                        out=sc0[:, mc * T:(mc + 1) * T], in_=pb[bkn][:, 0:T],
                        func=AF.Silu, bias=b1s[:, m:m + 1]), "S")
                    bank_free[bkn] = ("S", s_silu)
                p2_pump()
                wait("vector", "S", s_silu)
                wait_tok("vector", dq_state["laststore"])
                emit("vector", lambda e, fg=fg: e.tensor_copy(
                    out=y0[:, fg * 2048:(fg + 1) * 2048], in_=sc0))
                v_ysplit = emit("vector", lambda e, fg=fg: e.tensor_tensor(
                    out=y1[:, fg * 2048:(fg + 1) * 2048], in0=sc0,
                    in1=y0[:, fg * 2048:(fg + 1) * 2048], op=ALU.subtract), "V")
                v_yq[fg] = v_ysplit
                if t == 0 and fg == 0:
                    tap("dy", sc0, "V", v_ysplit)

            if t + 1 < NT:
                p2_vstage(t + 1)

            fc2_last = [None] * HC
            for kg in range(4):
                key = f"w2g{kg}"
                sl = L[key + "_slot"]
                wh, wl = ws[sl][:, 0:8192], ws[sl][:, 8192:16384]
                wait("tensor", "V", v_yq[kg])
                wait_tok("tensor", L[key])
                for mc in range(HC):
                    if kg == 0:
                        use_bank(mc)
                    for kc in range(HC):
                        fkc = kg * 8 + kc
                        for pi, (wop, yop) in enumerate(((wh, y0), (wh, y1), (wl, y0))):
                            st = (kg == 0 and kc == 0 and pi == 0)
                            sp = (kg == 3 and kc == HC - 1 and pi == 2)
                            sec_end = (mc == HC - 1 and kc == HC - 1 and pi == 2)
                            r_ = emit("tensor", lambda e, wop=wop, yop=yop, kc=kc, mc=mc, fkc=fkc, st=st, sp=sp: e.matmul(
                                pb[mc][:, 0:T],
                                wop[:, kc * H + mc * P: kc * H + (mc + 1) * P],
                                yop[:, fkc * T:(fkc + 1) * T], start=st, stop=sp),
                                "T" if (sp or sec_end) else None)
                            if sp:
                                fc2_last[mc] = r_ if r_ is not None else cnt["T"]
                ws_last_mm[sl] = cnt["T"]
                p2_pump()
            v_tp = None
            for mc in range(HC):
                wait("vector", "T", fc2_last[mc])
                emit("vector", lambda e, mc=mc: e.tensor_scalar(
                    out=mus[:, :], in0=pb[mc][:, 0:T], scalar1=0.25,
                    scalar2=b2q[:, mc:mc + 1], op0=ALU.mult, op1=ALU.add))
                v_tp = emit("vector", lambda e, mc=mc, x1=x1, r1=r1: e.tensor_tensor(
                    out=r1[:, mc * T:(mc + 1) * T], in0=mus[:, :],
                    in1=x1[:, mc * T:(mc + 1) * T], op=ALU.add), "V")
                bank_free[mc] = ("V", v_tp)
            if t == 0:
                tap("dtp", r1, "V", v_tp)

            wait("scalar", "V", v_tp)
            wait("tensor", "V", v_tp)
            last_ln3_c2[0] = ln_stats(r1, sc1, 1)
            emit("vector", lambda e, r1=r1: e.tensor_tensor(
                out=r3(sc2, HC), in0=r3(r1, HC), in1=bc_chunk(mus), op=ALU.subtract))
            v_ofm = emit("vector", lambda e, xf=xf: e.tensor_tensor(
                out=r3(xf, HC), in0=r3(sc2, HC), in1=bc_chunk(rss), op=ALU.mult), "V")

            if t + 1 < NT:
                p2_wo(t + 1)
            p2_loads[t]["v_ofm"] = v_ofm
            if t == NT - 1:
                p2_out(t)

        wait_tok("gpsimd", qsnap("GQ"))
        wait_tok("gpsimd", qsnap("SQ"))

        def replay(name):
            def run(eng):
                for item in ops[name]:
                    if item[0] == "wait":
                        eng.wait_ge(sems[item[1]], item[2])
                    else:
                        inst = item[1](eng)
                        if item[2]:
                            inst.then_inc(sems[item[2]], item[3])
            return run

        block.tensor(replay("tensor"))
        block.vector(replay("vector"))
        block.scalar(replay("scalar"))
        block.sync(replay("sync"))
        block.gpsimd(replay("gpsimd"))

    return nc


def kernel(L0_attn_x, L0_attn_Wq, L0_attn_bq, L0_attn_Wk, L0_attn_bk,
           L0_attn_Wv, L0_attn_bv, L0_attn_Wo, L0_attn_bo,
           L0_ffn_ln_gamma, L0_ffn_ln_beta, L0_ffn_fc1_q, L0_ffn_fc1_scales,
           L0_ffn_fc1_bias, L0_ffn_fc2_q, L0_ffn_fc2_scales, L0_ffn_fc2_bias):
    x = np.ascontiguousarray(np.asarray(L0_attn_x, dtype=np.float32))
    common = dict(
        wq=np.ascontiguousarray(np.asarray(L0_attn_Wq, np.float32)),
        wk=np.ascontiguousarray(np.asarray(L0_attn_Wk, np.float32)),
        wv=np.ascontiguousarray(np.asarray(L0_attn_Wv, np.float32)),
        wo=np.ascontiguousarray(np.asarray(L0_attn_Wo, np.float32)),
        bq=np.ascontiguousarray(np.asarray(L0_attn_bq, np.float32)),
        bk=np.ascontiguousarray(np.asarray(L0_attn_bk, np.float32)),
        bv=np.ascontiguousarray(np.asarray(L0_attn_bv, np.float32)),
        bo=np.ascontiguousarray(np.asarray(L0_attn_bo, np.float32)),
        gm=np.ascontiguousarray(np.asarray(L0_ffn_ln_gamma, np.float32)),
        bt=np.ascontiguousarray(np.asarray(L0_ffn_ln_beta, np.float32)),
        f1q=np.ascontiguousarray(np.asarray(L0_ffn_fc1_q, np.int32)),
        f1s=np.ascontiguousarray(np.asarray(L0_ffn_fc1_scales, np.float32)),
        f1b=np.ascontiguousarray(np.asarray(L0_ffn_fc1_bias, np.float32)),
        f2q=np.ascontiguousarray(np.asarray(L0_ffn_fc2_q, np.int32)),
        f2s=np.ascontiguousarray(np.asarray(L0_ffn_fc2_scales, np.float32)),
        f2b=np.ascontiguousarray(np.asarray(L0_ffn_fc2_bias, np.float32)),
    )
    in_maps = [dict(common, x=np.ascontiguousarray(x[c])) for c in range(8)]
    if _NC_CACHE[0] is None:
        _NC_CACHE[0] = build_nc()
    res = run_bass_kernel_spmd(_NC_CACHE[0], in_maps, core_ids=list(range(8)))
    return np.stack([res.results[c]["out"] for c in range(8)], axis=0)
